# version 13
# baseline (speedup 1.0000x reference)
import numpy as np
import ml_dtypes
import concourse.bass as bass
import concourse.mybir as mybir
from concourse.bass_utils import run_bass_kernel_spmd

F32 = mybir.dt.float32
BF16 = mybir.dt.bfloat16
I32 = mybir.dt.int32
AF = mybir.ActivationFunctionType
ALU = mybir.AluOpType
AX = mybir.AxisListType

DM = 2048
S = 4096
B = 4
T_OWN = 2048
NH_MY = 8
D = 128
NKV_MY = 2
IC = 256
N_IC = 8192 // IC
SC = 0.08838834764831845
NEG = -2.3819763e+38
EPS = 1e-5

C16 = DM // 128


class Prog:
    ENGS = ("pe", "act", "dve", "pool", "sp")

    def __init__(self, nc, sems, slot_sems):
        self.nc = nc
        self.sem = dict(sems)
        for qn, d in slot_sems.items():
            self.sem.update(d)
        self.slot_names = {qn: list(d.keys()) for qn, d in slot_sems.items()}
        self.cnt = {e: 0 for e in self.ENGS}
        self.slot_cnt = {s: 0 for d in slot_sems.values() for s in d}
        self.slot_rr = {qn: 0 for qn in slot_sems}
        self.insts = {e: [] for e in self.ENGS}
        self.regions = {}
        self.water = {}
        self.rt = {}

    def _waits_for(self, eng, reads, writes):
        waits = []

        def need(dep):
            sname, val = dep
            if sname == eng == "pe":
                return
            w = self.water.get((eng, sname), 0)
            if val > w:
                waits.append((sname, val))
                self.water[(eng, sname)] = val
        for k in reads:
            r = self.regions.get(k)
            if r and r[0]:
                need(r[0])
        for k in writes:
            r = self.regions.get(k)
            if r:
                if r[0]:
                    need(r[0])
                for rd in r[1]:
                    need(rd)
        return waits

    def _commit(self, dep, reads, writes):
        for k in writes:
            self.regions[k] = [dep, []]
        for k in reads:
            self.regions.setdefault(k, [None, []])[1].append(dep)

    def op(self, eng, fn, reads=(), writes=()):
        waits = self._waits_for(eng, reads, writes)
        self.cnt[eng] += 1
        dep = (eng, self.cnt[eng])
        self.insts[eng].append((waits, fn, (eng, 1)))
        self._commit(dep, reads, writes)
        return dep

    def raw(self, eng, fn):
        self.insts[eng].append(([], fn, None))

    def wait(self, eng, dep):
        sname, val = dep
        w = self.water.get((eng, sname), 0)
        if val > w:
            self.insts[eng].append(([(sname, val)], None, None))
            self.water[(eng, sname)] = val

    def barrier_dep(self, key):
        r = self.regions.get(key)
        return r[0] if r else None

    def fence(self, extra=(), light=()):
        comp = [(e, self.cnt[e]) for e in self.ENGS if self.cnt[e] > 0]
        slots = [(s, c) for s, c in self.slot_cnt.items() if c > 0]
        comp += list(extra)
        for e in self.ENGS:
            deps = comp if e in light else comp + slots
            for dep in deps:
                if dep[0] == e:
                    continue
                self.wait(e, dep)

    def dma(self, queue, fn, reads=(), writes=()):
        eng = queue
        names = self.slot_names[queue]
        slot = names[self.slot_rr[queue] % len(names)]
        self.slot_rr[queue] += 1
        waits = []
        prev = self.slot_cnt[slot]
        if prev:
            w = self.water.get((eng, slot), 0)
            if prev > w:
                waits.append((slot, prev))
                self.water[(eng, slot)] = prev
        waits += self._waits_for(eng, reads, writes)
        self.slot_cnt[slot] += 16
        dep = (slot, self.slot_cnt[slot])
        self.insts[eng].append((waits, fn, (slot, 16)))
        self._commit(dep, reads, writes)
        return dep

    def replay(self, block):
        deco = {"pe": block.tensor, "act": block.scalar, "dve": block.vector,
                "pool": block.gpsimd, "sp": block.sync}
        for ename in self.ENGS:
            ilist = self.insts[ename]

            def body(eng, ilist=ilist):
                for waits, fn, inc in ilist:
                    for sname, val in waits:
                        eng.wait_ge(self.sem[sname], val)
                    if fn is None:
                        continue
                    ins = fn(eng)
                    if inc is not None and ins is not None:
                        ins.then_inc(self.sem[inc[0]], inc[1])
            deco[ename](body)


def _build_nc(sim_single=False):
    from contextlib import ExitStack
    nc = bass.Bass("TRN2", target_bir_lowering=False, num_devices=8)

    x_full = nc.dram_tensor("x_full", [S, DM], F32, kind="ExternalInput").ap()
    x_own = nc.dram_tensor("x_own", [T_OWN, DM], F32, kind="ExternalInput").ap()
    wq_in = nc.dram_tensor("wq_my", [DM, NH_MY * D], F32, kind="ExternalInput").ap()
    wk_in = nc.dram_tensor("wk_my", [DM, NKV_MY * D], F32, kind="ExternalInput").ap()
    wv_in = nc.dram_tensor("wv_my", [DM, NKV_MY * D], F32, kind="ExternalInput").ap()
    wo_in = nc.dram_tensor("wo", [DM, DM], F32, kind="ExternalInput").ap()
    wg_in = nc.dram_tensor("w_gate", [16, 128, C16 * 512], F32,
                           kind="ExternalInput").ap()
    wu_in = nc.dram_tensor("w_up", [16, 128, C16 * 512], F32,
                           kind="ExternalInput").ap()
    wd_in = nc.dram_tensor("w_down", [8192, DM], F32, kind="ExternalInput").ap()
    rope_in = nc.dram_tensor("rope", [64, 2 * S], BF16, kind="ExternalInput").ap()
    mask_in = nc.dram_tensor("masks", [128, 4 * 512], BF16, kind="ExternalInput").ap()
    id_in = nc.dram_tensor("ident", [128, 128], BF16, kind="ExternalInput").ap()
    qb_in = nc.dram_tensor("qbase", [1, 1], I32, kind="ExternalInput").ap()
    pb_in = nc.dram_tensor("pbase", [1, 1], I32, kind="ExternalInput").ap()
    sb_in = nc.dram_tensor("sbase", [1, 1], I32, kind="ExternalInput").ap()
    out = nc.dram_tensor("out", [T_OWN, DM], F32, kind="ExternalOutput").ap()

    q_dram = nc.dram_tensor("q_dram", [NH_MY * D, S], BF16, kind="Internal").ap()
    o_my = nc.dram_tensor("o_my", [NH_MY * D, S], BF16, kind="Internal").ap()
    o_allh = [nc.dram_tensor(f"o_allh{m}", [2 * D, S], BF16,
                             kind="Internal").ap() for m in range(NH_MY)]
    xn_dram = nc.dram_tensor("xn_dram", [8 * 128, C16 * 512], BF16,
                             kind="Internal").ap()
    mid = nc.dram_tensor("mid", [T_OWN, DM], F32, kind="Internal").ap()

    ctx = ExitStack()
    E = ctx.enter_context

    sems = {e: E(nc.semaphore("sem_" + e)) for e in Prog.ENGS}
    slot_sems = {
        "sp": {f"dqs{i}": E(nc.semaphore(f"dqs{i}")) for i in range(12)},
        "pool": {f"dqp{i}": E(nc.semaphore(f"dqp{i}")) for i in range(12)},
        "act": {f"dqa{i}": E(nc.semaphore(f"dqa{i}")) for i in range(8)},
    }
    cc_sem = E(nc.semaphore("cc_sem"))
    p = Prog(nc, sems, slot_sems)

    ident_sb = E(nc.sbuf_tensor("ident_sb", [128, 128], BF16))
    ones_col = E(nc.sbuf_tensor("ones_col", [128, 1], BF16))
    ones_row = E(nc.sbuf_tensor("ones_row", [1, 128], BF16))
    eps_col = E(nc.sbuf_tensor("eps_col", [128, 1], F32))
    qb_sb = E(nc.sbuf_tensor("qb_sb", [1, 1], I32))
    pb_sb = E(nc.sbuf_tensor("pb_sb", [1, 1], I32))
    sb_sb = E(nc.sbuf_tensor("sb_sb", [1, 1], I32))
    qreg = E(nc.sync.register("qreg"))
    preg = E(nc.sync.register("preg"))
    sreg = E(nc.sync.register("sreg"))

    p.dma("sp", lambda e: e.dma_start(out=ident_sb[:, :], in_=id_in[:, :]),
          writes=["ident"])
    p.dma("sp", lambda e: e.dma_start(out=qb_sb[:, :], in_=qb_in[:, :]),
          writes=["qb"])
    p.dma("sp", lambda e: e.dma_start(out=pb_sb[:, :], in_=pb_in[:, :]),
          writes=["pb"])
    p.dma("sp", lambda e: e.dma_start(out=sb_sb[:, :], in_=sb_in[:, :]),
          writes=["sb"])
    p.op("dve", lambda e: e.memset(ones_col[:, :], 1.0), writes=["ones_c"])
    p.op("dve", lambda e: e.memset(ones_row[:, :], 1.0), writes=["ones_r"])
    p.op("dve", lambda e: e.memset(eps_col[:, :], EPS), writes=["eps"])

    p.op("sp", lambda e: e.reg_load(qreg, qb_sb[:1, :1]), reads=["qb"])
    p.op("sp", lambda e: e.reg_load(preg, pb_sb[:1, :1]), reads=["pb"])
    p.op("sp", lambda e: e.reg_load(sreg, sb_sb[:1, :1]), reads=["sb"])

    def do_snap(e):
        p.rt["qoff"] = e.snap(qreg, min_val=0, max_val=T_OWN)
        p.rt["pbh"] = e.snap(preg, min_val=0, max_val=D)
        p.rt["sb"] = e.snap(sreg, min_val=0, max_val=512)
        return None
    p.raw("sp", do_snap)

    cc_groups = [[0, 1], [2, 3], [4, 5], [6, 7]]
    p.sem["cc"] = cc_sem
    cc_cnt = [0]

    def emit_head_cc(m):
        keys = [("omy", m, st) for st in range(8)]
        if sim_single:
            p.dma("pool", lambda e, m=m: e.dma_start(
                out=o_allh[m][0:D, :], in_=o_my[m * D:(m + 1) * D, :]),
                reads=keys, writes=[("oalla", m)])
            p.dma("pool", lambda e, m=m: e.dma_start(
                out=o_allh[m][D:2 * D, :], in_=o_my[m * D:(m + 1) * D, :]),
                reads=keys, writes=[("oallb", m)])
            p.regions[("oall", m)] = [p.barrier_dep(("oallb", m)), []]
        else:
            for k in keys:
                d = p.barrier_dep(k)
                if d:
                    p.wait("pool", d)
            cc_cnt[0] += 1
            p.insts["pool"].append((
                [], lambda e, m=m: e.collective_compute(
                    "AllGather", ALU.bypass, replica_groups=cc_groups,
                    ins=[o_my[m * D:(m + 1) * D, :]], outs=[o_allh[m][:, :]]),
                ("cc", 1)))
            p.regions[("oall", m)] = [("cc", cc_cnt[0]), []]

    from contextlib import ExitStack as _ES
    ctx_kv = _ES()
    kg_sb = [ctx_kv.enter_context(nc.sbuf_tensor(f"kg{g}", [128, S], BF16))
             for g in range(2)]
    vg_sb = [ctx_kv.enter_context(nc.sbuf_tensor(f"vg{g}", [128, 32 * 128],
                                                 BF16)) for g in range(2)]
    ctx_wo = _ES()
    ctx_oo = _ES()

    with ExitStack() as c1:
        E1 = c1.enter_context
        rope_sb = E1(nc.sbuf_tensor("rope_sb", [64, 2 * S], BF16))
        wq_sb = E1(nc.sbuf_tensor("wq_sb", [128, C16 * NH_MY * D], BF16))
        wk_sb = E1(nc.sbuf_tensor("wk_sb", [128, C16 * NKV_MY * D], BF16))
        wv_sb = E1(nc.sbuf_tensor("wv_sb", [128, C16 * NKV_MY * D], BF16))
        xbuf = [E1(nc.sbuf_tensor(f"xbuf{i}", [128, DM], F32)) for i in range(2)]
        xn_tok = [E1(nc.sbuf_tensor(f"xn_tok{i}", [128, DM], BF16)) for i in range(4)]
        xn_str = [E1(nc.sbuf_tensor(f"xn_str{i}", [128, C16 * 512], BF16))
                  for i in range(2)]
        s1c = [E1(nc.sbuf_tensor(f"s1c{i}", [128, 1], F32)) for i in range(2)]
        nmc = [E1(nc.sbuf_tensor(f"nmc{i}", [128, 1], F32)) for i in range(2)]
        s2c = [E1(nc.sbuf_tensor(f"s2c{i}", [128, 1], F32)) for i in range(2)]
        varc = [E1(nc.sbuf_tensor(f"varc{i}", [128, 1], F32)) for i in range(2)]
        rstdc = [E1(nc.sbuf_tensor(f"rstdc{i}", [128, 1], F32)) for i in range(2)]
        sqdump = E1(nc.sbuf_tensor("sqdump", [128, DM], BF16, side="right"))
        q_stage = [E1(nc.sbuf_tensor(f"qst{i}", [128, 512], BF16)) for i in range(2)]
        rr = [[E1(nc.sbuf_tensor(f"rr{i}_{j}", [64, 512], F32, side="right"))
               for j in range(4)] for i in range(2)]
        ps_a = [E1(nc.psum_tensor(f"ps_a{i}", [128, 512], F32)) for i in range(2)]
        ps_v = [E1(nc.psum_tensor(f"ps_v{i}", [128, NKV_MY * D], F32))
                for i in range(2)]
        ps_tr = [E1(nc.psum_tensor(f"ps_tr{i}", [128, 512], BF16)) for i in range(2)]

        p.dma("sp", lambda e: e.dma_start(out=rope_sb[:, :], in_=rope_in[:, :]),
              writes=["rope"])
        for c in range(C16):
            p.dma("pool", lambda e, c=c: e.dma_start(
                out=wq_sb[:, c * NH_MY * D:(c + 1) * NH_MY * D],
                in_=wq_in[c * 128:(c + 1) * 128, :]), writes=[("wq", c)])
            p.dma("pool", lambda e, c=c: e.dma_start(
                out=wk_sb[:, c * NKV_MY * D:(c + 1) * NKV_MY * D],
                in_=wk_in[c * 128:(c + 1) * 128, :]), writes=[("wk", c)])
            p.dma("pool", lambda e, c=c: e.dma_start(
                out=wv_sb[:, c * NKV_MY * D:(c + 1) * NKV_MY * D],
                in_=wv_in[c * 128:(c + 1) * 128, :]), writes=[("wv", c)])

        def ln_tile(t, xb, dst_key, dst_tok):
            i = xb
            p.op("dve", lambda e: e.reduce_sum(s1c[i][:, :], xbuf[i][:, :],
                                               axis=AX.X),
                 reads=[("xb", i)], writes=[("s1", i)])
            p.op("dve", lambda e: e.tensor_scalar_mul(nmc[i][:, :], s1c[i][:, :],
                                                      -1.0 / DM),
                 reads=[("s1", i)], writes=[("nm", i)])
            p.op("act", lambda e: e.activation(sqdump[:, :], xbuf[i][:, :], AF.Square,
                                               accum_out=s2c[i][:, :]),
                 reads=[("xb", i)], writes=["sqd", ("s2", i)])
            p.op("dve", lambda e: e.tensor_tensor(varc[i][:, :], nmc[i][:, :],
                                                  nmc[i][:, :], ALU.mult),
                 reads=[("nm", i)], writes=[("var", i)])
            p.op("dve", lambda e: e.tensor_scalar(varc[i][:, :], s2c[i][:, :],
                                                  1.0 / DM, varc[i][:, :],
                                                  ALU.mult, ALU.subtract),
                 reads=[("s2", i), ("var", i)], writes=[("var", i)])
            p.op("act", lambda e: e.activation(rstdc[i][:, :], varc[i][:, :], AF.Ln,
                                               bias=eps_col[:, :]),
                 reads=[("var", i), "eps"], writes=[("rstd", i)])
            p.op("act", lambda e: e.activation(rstdc[i][:, :], rstdc[i][:, :], AF.Exp,
                                               scale=-0.5),
                 reads=[("rstd", i)], writes=[("rstd", i)])
            p.op("pool", lambda e, dt=dst_tok: e.tensor_scalar(
                xn_tok[dt][:, :], xbuf[i][:, :], nmc[i][:, :], rstdc[i][:, :],
                ALU.add, ALU.mult),
                 reads=[("xb", i), ("nm", i), ("rstd", i)], writes=[dst_key])

        def transpose_tile(i, tt, str_buf):
            for g in range(4):
                pb = (tt * 4 + g) % 2
                for j in range(4):
                    c = g * 4 + j
                    p.op("pe", lambda e, c=c, j=j, pb=pb, i=i: e.transpose(
                        ps_tr[pb][:, j * 128:(j + 1) * 128],
                        xn_tok[i][:, c * 128:(c + 1) * 128], ident_sb[:, :]),
                        reads=[("xntok", i), "ident"], writes=[("pstr", pb)])
                c0 = g * 4
                p.op("act", lambda e, c0=c0, tt=tt, pb=pb, sbf=str_buf:
                     e.copy(xn_str[sbf][:, :].rearrange("p (c n) -> p c n", c=C16)
                            [:, c0:c0 + 4, tt * 128:(tt + 1) * 128],
                            ps_tr[pb][:, :].rearrange("p (c n) -> p c n", c=4)),
                     reads=[("pstr", pb)], writes=[("xns", str_buf, c0 + j, tt)
                                                   for j in range(4)])

        def rope_apply(ps, stage_top, stage_bot, s0, reads, wtop, wbot, pq):
            cosv = rope_sb[0:64, s0:s0 + 512]
            sinv = rope_sb[0:64, S + s0:S + s0 + 512]
            A = ps[0:64, :]
            Bp = ps[64:128, :]
            for j, (src_, trig) in enumerate(
                    ((A, cosv), (Bp, sinv), (A, sinv), (Bp, cosv))):
                p.op("dve", lambda e, j=j, src_=src_, trig=trig: e.tensor_tensor(
                    rr[pq][j][:, :], src_, trig, ALU.mult),
                    reads=list(reads) + ["rope"], writes=[("rr", pq, j)])
            p.op("pool", lambda e, st_=stage_top: e.tensor_tensor(
                st_, rr[pq][0][:, :], rr[pq][1][:, :], ALU.subtract),
                reads=[("rr", pq, 0), ("rr", pq, 1)], writes=wtop)
            p.op("pool", lambda e, sb_=stage_bot: e.tensor_tensor(
                sb_, rr[pq][2][:, :], rr[pq][3][:, :], ALU.add),
                reads=[("rr", pq, 2), ("rr", pq, 3)], writes=wbot)

        def ln_parts(st):
            for tt in range(4):
                t = st * 4 + tt
                i = t % 2
                p.dma("sp", lambda e, t=t, i=i: e.dma_start(
                    out=xbuf[i][:, :], in_=x_full[t * 128:(t + 1) * 128, :]),
                    writes=[("xb", i)])
                ln_tile(t, i, ("xntok", tt), tt)

        def transposes(st):
            sbf = st % 2
            for tt in range(4):
                transpose_tile(tt, tt, sbf)
            p.dma("act", lambda e, st=st, sbf=sbf: e.dma_start(
                out=xn_dram[st * 128:(st + 1) * 128, :], in_=xn_str[sbf][:, :]),
                reads=[("xns", sbf, c, tt) for c in range(C16)
                       for tt in range(4)],
                writes=[("xnd", st)])

        def projections(st):
            sbf = st % 2
            s0 = st * 512
            xs_keys = [("xns", sbf, c, tt) for c in range(C16) for tt in range(4)]
            for m in range(NH_MY):
                pq = m % 2
                for c in range(C16):
                    p.op("pe", lambda e, m=m, c=c, pq=pq, sbf=sbf: e.matmul(
                        ps_a[pq][:, :],
                        wq_sb[:, c * NH_MY * D + m * D: c * NH_MY * D + (m + 1) * D],
                        xn_str[sbf][:, c * 512:(c + 1) * 512],
                        start=(c == 0), stop=(c == C16 - 1)),
                        reads=[("wq", c)] + xs_keys, writes=[("psa", pq)])
                rope_apply(ps_a[pq], q_stage[pq][0:64, :], q_stage[pq][64:128, :],
                           s0, reads=[("psa", pq)], wtop=[("qsT", pq)],
                           wbot=[("qsB", pq)], pq=pq)
                p.dma("sp", lambda e, m=m, pq=pq, s0=s0: e.dma_start(
                    out=q_dram[m * 128:(m + 1) * 128, s0:s0 + 512],
                    in_=q_stage[pq][:, :]),
                    reads=[("qsT", pq), ("qsB", pq)], writes=[("qd", m, st)])
            for g in range(NKV_MY):
                pq = g % 2
                for c in range(C16):
                    p.op("pe", lambda e, g=g, c=c, pq=pq, sbf=sbf: e.matmul(
                        ps_a[pq][:, :],
                        wk_sb[:, c * NKV_MY * D + g * D: c * NKV_MY * D + (g + 1) * D],
                        xn_str[sbf][:, c * 512:(c + 1) * 512],
                        start=(c == 0), stop=(c == C16 - 1)),
                        reads=[("wk", c)] + xs_keys, writes=[("psa", pq)])
                rope_apply(ps_a[pq], kg_sb[g][0:64, s0:s0 + 512],
                           kg_sb[g][64:128, s0:s0 + 512], s0,
                           reads=[("psa", pq)], wtop=[("kgT", g, st)],
                           wbot=[("kgB", g, st)], pq=pq)
            for tt in range(4):
                pv = tt % 2
                jt = st * 4 + tt
                for c in range(C16):
                    p.op("pe", lambda e, tt=tt, c=c, pv=pv, sbf=sbf: e.matmul(
                        ps_v[pv][:, :],
                        xn_str[sbf][:, c * 512 + tt * 128: c * 512 + (tt + 1) * 128],
                        wv_sb[:, c * NKV_MY * D:(c + 1) * NKV_MY * D],
                        start=(c == 0), stop=(c == C16 - 1)),
                        reads=[("wv", c)] + xs_keys, writes=[("psv", pv)])
                for g in range(NKV_MY):
                    p.op("act", lambda e, pv=pv, g=g, jt=jt: e.copy(
                        vg_sb[g][:, jt * 128:(jt + 1) * 128],
                        ps_v[pv][:, g * 128:(g + 1) * 128]),
                        reads=[("psv", pv)], writes=[("vg", g, jt)])

        ln_parts(0)
        transposes(0)
        for st in range(8):
            if st + 1 < 8:
                ln_parts(st + 1)
            projections(st)
            if st + 1 < 8:
                transposes(st + 1)

    p.fence()
    wo_sb = ctx_wo.enter_context(
        nc.sbuf_tensor("wo_sb", [128, C16 * DM], BF16, side="right"))
    oo_sb = ctx_oo.enter_context(
        nc.sbuf_tensor("oo_sb", [128, C16 * T_OWN], BF16,
                       side="right"))
    for c in range(C16):
        p.dma("pool", lambda e, c=c: e.dma_start(
            out=wo_sb[:, c * DM:(c + 1) * DM],
            in_=wo_in[c * 128:(c + 1) * 128, :]), writes=[("wo", c)])

    with ExitStack() as c2:
        E2 = c2.enter_context
        m01_sb = E2(nc.sbuf_tensor("m01_sb", [128, 4 * 512], BF16))
        pt_sb = [E2(nc.sbuf_tensor(f"pt{i}", [128, 1024], BF16)) for i in range(3)]
        l2d = [E2(nc.sbuf_tensor(f"l2d{i}", [128, 1024], F32)) for i in range(2)]
        l2p = [E2(nc.sbuf_tensor(f"l2p{i}", [128, 1024], F32)) for i in range(2)]
        tmp_l = E2(nc.sbuf_tensor("tmp_l", [128, 512], F32))
        l_bf = [E2(nc.sbuf_tensor(f"lbf{i}", [128, 512], BF16)) for i in range(2)]
        rtmp = E2(nc.sbuf_tensor("rtmp", [1, 512], F32))
        rl_row = E2(nc.sbuf_tensor("rl_row", [1, 512], BF16))
        rl_sb = E2(nc.sbuf_tensor("rl_sb", [128, 512], F32))
        o_stage = [E2(nc.sbuf_tensor(f"ost{i}", [128, 512], BF16)) for i in range(2)]
        qbuf = [E2(nc.sbuf_tensor(f"qbuf{i}", [128, 512], BF16)) for i in range(2)]
        ps_w = [E2(nc.psum_tensor(f"ps_w{i}", [128, 1024], F32)) for i in range(3)]
        ps_att = [E2(nc.psum_tensor(f"ps_att{i}", [128, 512], F32)) for i in range(2)]

        p.dma("sp", lambda e: e.dma_start(out=m01_sb[:, :], in_=mask_in[:, :]),
              writes=["masks"])

        ST_ORD = [7, 0, 6, 1, 5, 2, 4, 3]
        iters = [(g, g * 4 + ml, st, si == 7) for g in range(NKV_MY)
                 for ml in range(4) for si, st in enumerate(ST_ORD)]

        def q_load(i):
            _, m, st, _l = iters[i]
            qb2 = i % 2
            p.dma("sp", lambda e, m=m, st=st, qb2=qb2: e.dma_start(
                out=qbuf[qb2][:, :],
                in_=q_dram[m * 128:(m + 1) * 128, st * 512:(st + 1) * 512]),
                reads=[("qd", m, st)], writes=[("qbuf", qb2)])

        def make_fsteps(m, st, ab, last):
            def sA(_):
                p.op("dve", lambda e, ab=ab: e.tensor_tensor(
                    tmp_l[:, :], l2d[ab][:, 0:512], l2d[ab][:, 512:1024],
                    ALU.add),
                    reads=[("l2d", ab)], writes=["tmpl"])
                p.op("pool", lambda e, ab=ab: e.tensor_tensor(
                    l2p[ab][:, 0:512], l2p[ab][:, 0:512], l2p[ab][:, 512:1024],
                    ALU.add),
                    reads=[("l2p", ab)], writes=[("l2p", ab)])
                p.op("dve", lambda e, ab=ab: e.tensor_tensor(
                    l_bf[ab][:, :], tmp_l[:, :], l2p[ab][:, 0:512], ALU.add),
                    reads=["tmpl", ("l2p", ab)], writes=[("lbf", ab)])

            def sB1(slot):
                p.op("pe", lambda e, ab=ab, slot=slot: e.matmul(
                    ps_w[slot][0:1, 0:512], ones_col[:, :], l_bf[ab][:, :],
                    start=True, stop=True),
                    reads=[("lbf", ab), "ones_c"], writes=[("psw", slot)])
                p.op("act", lambda e, slot=slot: e.activation(
                    rtmp[:, :], ps_w[slot][0:1, 0:512], AF.Ln),
                    reads=[("psw", slot)], writes=["rtmp"])

                def _exp(e):
                    with nc.allow_low_precision("1/l feeds bf16 matmul"):
                        return e.activation(rl_row[:, :], rtmp[:, :], AF.Exp,
                                            scale=-1.0)
                p.op("act", _exp, reads=["rtmp"], writes=["rlr"])

            def sB2(slot):
                p.op("pe", lambda e, slot=slot: e.matmul(
                    ps_w[slot][:, 0:512], ones_row[:, :], rl_row[:, :],
                    start=True, stop=True),
                    reads=["rlr", "ones_r"], writes=[("psw", slot)])
                p.op("dve", lambda e, slot=slot: e.tensor_copy(
                    rl_sb[:, :], ps_w[slot][:, 0:512]),
                    reads=[("psw", slot)], writes=["rlsb"])

            def sC(_):
                p.op("dve", lambda e, ab=ab: e.tensor_tensor(
                    o_stage[ab][:, :], ps_att[ab][:, :], rl_sb[:, :], ALU.mult),
                    reads=[("psatt", ab), "rlsb"], writes=[("osta", ab)])
                p.dma("sp", lambda e, ab=ab, m=m, st=st: e.dma_start(
                    out=o_my[m * 128:(m + 1) * 128, st * 512:(st + 1) * 512],
                    in_=o_stage[ab][:, :]),
                    reads=[("osta", ab)], writes=[("omy", m, st)])
                if last:
                    cl = m
                    p.dma("sp", lambda e, cl=cl: e.dma_start(
                        out=oo_sb[:, cl * T_OWN:(cl + 1) * T_OWN],
                        in_=o_my[cl * 128:(cl + 1) * 128,
                                 bass.ds(p.rt["qoff"], T_OWN)]),
                        reads=[("omy", cl, s2_) for s2_ in range(8)],
                        writes=[("oo", cl)])
                    emit_head_cc(cl)
                    p.dma("sp", lambda e, cl=cl: e.dma_start(
                        out=oo_sb[:, (8 + cl) * T_OWN:(9 + cl) * T_OWN],
                        in_=o_allh[cl][bass.ds(p.rt["pbh"], 128),
                                       bass.ds(p.rt["qoff"], T_OWN)]),
                        reads=[("oall", cl)], writes=[("oo", 8 + cl)])
            sA(None)
            return [sB1, sB2, sC]

        fsteps = []
        q_load(0)
        for it, (g, m, st, last) in enumerate(iters):
            qb2 = it % 2
            ab = it % 2
            nkt = 4 * (st + 1)
            npairs = nkt // 2
            if it + 1 < len(iters):
                q_load(it + 1)

            def emit_pair(pp, g=g, qb2=qb2):
                pb = pp % 3
                for h in range(2):
                    kt = 2 * pp + h
                    p.op("pe", lambda e, kt=kt, h=h, pb=pb, g=g, qb2=qb2:
                         e.matmul(ps_w[pb][:, h * 512:(h + 1) * 512],
                                  kg_sb[g][:, kt * 128:(kt + 1) * 128],
                                  qbuf[qb2][:, :], start=True, stop=True),
                         reads=[("kgT", g, kt // 4), ("kgB", g, kt // 4),
                                ("qbuf", qb2)],
                         writes=[("psw", pb)])

            emit_pair(0)
            if npairs > 1:
                emit_pair(1)
            for pp in range(npairs):
                pb = pp % 3
                if pp + 2 < npairs:
                    emit_pair(pp + 2)
                p.op("act", lambda e, pb=pb: e.activation(
                    pt_sb[pb][:, :], ps_w[pb][:, :], AF.Exp, scale=SC),
                    reads=[("psw", pb)], writes=[("pt", pb)])
                if fsteps:
                    fsteps.pop(0)(pb)
                if pp >= npairs - 2:
                    v2 = pp - (npairs - 2)
                    p.op("dve", lambda e, pb=pb, v2=v2: e.tensor_tensor(
                        pt_sb[pb][:, :], pt_sb[pb][:, :],
                        m01_sb[:, v2 * 1024:(v2 + 1) * 1024], ALU.mult),
                        reads=[("pt", pb), "masks"], writes=[("pt", pb)])
                on_pool = (pp % 2 == 1)
                la = l2p[ab] if on_pool else l2d[ab]
                lk = ("l2p", ab) if on_pool else ("l2d", ab)
                leng = "pool" if on_pool else "dve"
                if pp <= 1:
                    p.op(leng, lambda e, pb=pb, la=la: e.tensor_copy(
                        la[:, :], pt_sb[pb][:, :]),
                        reads=[("pt", pb)], writes=[lk])
                else:
                    p.op(leng, lambda e, pb=pb, la=la: e.tensor_tensor(
                        la[:, :], la[:, :], pt_sb[pb][:, :], ALU.add),
                        reads=[("pt", pb), lk], writes=[lk])
                for h in range(2):
                    kt = 2 * pp + h
                    p.op("pe", lambda e, kt=kt, h=h, pb=pb, ab=ab, g=g,
                         nkt=nkt: e.matmul(
                             ps_att[ab][:, :],
                             vg_sb[g][:, kt * 128:(kt + 1) * 128],
                             pt_sb[pb][:, h * 512:(h + 1) * 512],
                             start=(kt == 0), stop=(kt == nkt - 1)),
                         reads=[("pt", pb), ("vg", g, kt)],
                         writes=[("psatt", ab)])
            while fsteps:
                fsteps.pop(0)((npairs - 1) % 3)
            fsteps = make_fsteps(m, st, ab, last)
        while fsteps:
            fsteps.pop(0)(0)

    ctx_kv.close()

    p.fence(light=("pe",))
    with ExitStack() as c3:
        E3 = c3.enter_context
        xo_buf = [E3(nc.sbuf_tensor(f"xo{i}", [128, DM], F32)) for i in range(2)]
        out_sb = [E3(nc.sbuf_tensor(f"outsb{i}", [128, 512], F32)) for i in range(4)]
        ps_o = [E3(nc.psum_tensor(f"ps_o{i}", [128, 512], F32)) for i in range(2)]

        for qt in range(16):
            xb = qt % 2
            p.dma("sp", lambda e, qt=qt, xb=xb: e.dma_start(
                out=xo_buf[xb][:, :], in_=x_own[qt * 128:(qt + 1) * 128, :]),
                writes=[("xo", xb)])
            for n in range(4):
                ob = n % 2
                for c in range(C16):
                    p.op("pe", lambda e, qt=qt, n=n, c=c, ob=ob: e.matmul(
                        ps_o[ob][:, :],
                        oo_sb[:, c * T_OWN + qt * 128: c * T_OWN + (qt + 1) * 128],
                        wo_sb[:, c * DM + n * 512: c * DM + (n + 1) * 512],
                        start=(c == 0), stop=(c == C16 - 1)),
                        reads=[("oo", c), ("wo", c)], writes=[("pso", ob)])
                p.op("dve", lambda e, n=n, ob=ob, xb=xb: e.tensor_tensor(
                    out_sb[n][:, :], ps_o[ob][:, :],
                    xo_buf[xb][:, n * 512:(n + 1) * 512], ALU.add),
                    reads=[("pso", ob), ("xo", xb)], writes=[("outsb", n)])
                p.dma("act", lambda e, qt=qt, n=n: e.dma_start(
                    out=mid[qt * 128:(qt + 1) * 128, n * 512:(n + 1) * 512],
                    in_=out_sb[n][:, :]),
                    reads=[("outsb", n)], writes=[("mid", qt, n)])

    ctx_oo.close()
    ctx_wo.close()

    p.fence()
    with ExitStack() as c4:
        E4 = c4.enter_context
        acc = E4(nc.sbuf_tensor("acc", [128, 8 * DM], F32))
        xnh = E4(nc.sbuf_tensor("xnh", [128, C16 * 1024], BF16))
        hspt = [E4(nc.sbuf_tensor(f"hspt{q}", [128, 4 * 128], BF16))
                for q in range(8)]
        hg = [E4(nc.sbuf_tensor(f"hg{i}", [128, 512], F32)) for i in range(2)]
        hh = [E4(nc.sbuf_tensor(f"hh{i}", [128, 512], F32)) for i in range(2)]
        aa = [E4(nc.sbuf_tensor(f"aa{i}", [128, 512], F32)) for i in range(2)]
        mk = [E4(nc.sbuf_tensor(f"mk{i}", [128, 512], F32)) for i in range(2)]
        hsp = [E4(nc.sbuf_tensor(f"hsp{i}", [128, 512], BF16)) for i in range(2)]
        thr = [E4(nc.sbuf_tensor(f"thr{i}", [128, 128], F32)) for i in range(2)]
        t1 = [E4(nc.sbuf_tensor(f"t1{i}", [128, 128], F32)) for i in range(2)]
        t2 = [E4(nc.sbuf_tensor(f"t2{i}", [128, 128], F32)) for i in range(2)]
        ps_g = E4(nc.psum_tensor("ps_g", [128, 512], F32))
        ps_u = E4(nc.psum_tensor("ps_u", [128, 512], F32))
        ps_tr4 = [E4(nc.psum_tensor(f"ps_tr4{i}", [128, 512], BF16))
                  for i in range(2)]
        ps_d = [E4(nc.psum_tensor(f"ps_d{n}", [128, 512], F32)) for n in range(4)]

        def chain(q, pr):
            i = q % 2
            p.op("act", lambda e, i=i: e.activation(hg[i][:, :], ps_g[:, :],
                                                    AF.Silu),
                 reads=["psg"], writes=[("hg", i)])
            p.op("dve", lambda e, i=i: e.tensor_tensor(hh[i][:, :], hg[i][:, :],
                                                       ps_u[:, :], ALU.mult),
                 reads=[("hg", i), "psu"], writes=[("hh", i)])
            p.op("act", lambda e, i=i: e.activation(aa[i][:, :], hh[i][:, :],
                                                    AF.Abs),
                 reads=[("hh", i)], writes=[("aa", i)])
            a3 = aa[i].ap().rearrange("p (g j) -> p g j", j=4)
            p.op("dve", lambda e, i=i, a3=a3: e.tensor_tensor(
                t1[i][:, :], a3[:, :, 0], a3[:, :, 1], ALU.max),
                reads=[("aa", i)], writes=[("t1", i)])
            p.op("dve", lambda e, i=i, a3=a3: e.tensor_tensor(
                t2[i][:, :], a3[:, :, 0], a3[:, :, 1], ALU.min),
                reads=[("aa", i)], writes=[("t2", i)])
            p.op("dve", lambda e, i=i, a3=a3: e.tensor_tensor(
                thr[i][:, :], a3[:, :, 2], a3[:, :, 3], ALU.max),
                reads=[("aa", i)], writes=[("thr", i)])
            p.op("dve", lambda e, i=i: e.tensor_tensor(
                t1[i][:, :], t1[i][:, :], thr[i][:, :], ALU.min),
                reads=[("t1", i), ("thr", i)], writes=[("t1", i)])
            p.op("dve", lambda e, i=i, a3=a3: e.tensor_tensor(
                thr[i][:, :], a3[:, :, 2], a3[:, :, 3], ALU.min),
                reads=[("aa", i)], writes=[("thr", i)])
            p.op("dve", lambda e, i=i: e.tensor_tensor(
                t2[i][:, :], t2[i][:, :], thr[i][:, :], ALU.max),
                reads=[("t2", i), ("thr", i)], writes=[("t2", i)])
            p.op("dve", lambda e, i=i: e.tensor_tensor(
                thr[i][:, :], t1[i][:, :], t2[i][:, :], ALU.max),
                reads=[("t1", i), ("t2", i)], writes=[("thr", i)])
            thr_b = thr[i].ap().unsqueeze(2).to_broadcast((128, 128, 4))
            p.op("dve", lambda e, i=i, a3=a3, thr_b=thr_b: e.tensor_tensor(
                mk[i].ap().rearrange("p (g j) -> p g j", j=4), a3, thr_b,
                ALU.is_ge),
                reads=[("aa", i), ("thr", i)], writes=[("mk", i)])
            p.op("dve", lambda e, i=i: e.tensor_tensor(
                hsp[i][:, :], hh[i][:, :], mk[i][:, :], ALU.mult),
                reads=[("hh", i), ("mk", i)], writes=[("hsp", i)])

        def transp_evac(q, pr):
            i = q % 2
            tb = q % 2
            for j in range(4):
                p.op("pe", lambda e, i=i, j=j, tb=tb: e.transpose(
                    ps_tr4[tb][:, j * 128:(j + 1) * 128],
                    hsp[i][:, j * 128:(j + 1) * 128], ident_sb[:, :]),
                    reads=[("hsp", i), "ident"], writes=[("pst4", tb)])
            p.op("act", lambda e, q=q, tb=tb: e.copy(hspt[q][:, :],
                                                     ps_tr4[tb][:, :]),
                 reads=[("pst4", tb)], writes=[("hspt", q)])

        def down_mms(q, pr):
            for n in range(4):
                for ii in range(4):
                    p.op("pe", lambda e, q=q, n=n, ii=ii: e.matmul(
                        ps_d[n][:, :],
                        hspt[q][:, ii * 128:(ii + 1) * 128],
                        d_sb[:, ii * DM + n * 512: ii * DM + (n + 1) * 512],
                        start=(ii == 0), stop=(ii == 3)),
                        reads=[("hspt", q), "dsb"],
                        writes=[("psd", n)])
                p.op("dve", lambda e, q=q, n=n: e.tensor_tensor(
                    acc[:, q * DM + n * 512: q * DM + (n + 1) * 512],
                    ps_d[n][:, :],
                    acc[:, q * DM + n * 512: q * DM + (n + 1) * 512],
                    ALU.add),
                    reads=[("psd", n), ("acc", q)], writes=[("acc", q)])

        for half in range(2):
            q0 = half * 8
            for q in range(8):
                p.dma("sp", lambda e, q=q, q0=q0: e.dma_start(
                    out=acc[:, q * DM:(q + 1) * DM],
                    in_=mid[(q0 + q) * 128:(q0 + q + 1) * 128, :]),
                    reads=[("mid", q0 + q, n) for n in range(4)],
                    writes=[("acc", q)])
            for tt in range(2):
                p.dma("sp", lambda e, tt=tt, half=half: e.dma_start(
                    out=xnh[:, :].rearrange("p (c n) -> p c n", c=C16)
                    [:, :, tt * 512:(tt + 1) * 512],
                    in_=xn_dram[bass.ds(p.rt["sb"] + (half * 2 + tt) * 128,
                                        128), :]
                    .rearrange("p (c n) -> p c n", c=C16)),
                    reads=[("xnd", st) for st in range(8)],
                    writes=[("xnh2", half, tt)])
            xnh_keys = [("xnh2", half, tt) for tt in range(2)]
            with ExitStack() as cs:
                ES = cs.enter_context
                g_sb = [ES(nc.sbuf_tensor(f"g2_{half}_{i}", [128, C16 * 512],
                                          BF16)) for i in range(2)]
                u_sb = [ES(nc.sbuf_tensor(f"u2_{half}_{i}", [128, C16 * 512],
                                          BF16)) for i in range(2)]
                d_sb = ES(nc.sbuf_tensor(f"d2_{half}", [128, 4 * DM], BF16))

                pr_seq = range(16) if half == 0 else range(15, -1, -1)
                for pr in pr_seq:
                    gb = pr % 2
                    if not (half == 1 and pr >= 14):
                        p.dma("pool", lambda e, gb=gb, pr=pr: e.dma_start(
                            out=g_sb[gb][:, :], in_=wg_in[pr]),
                            writes=[("gsb", gb)])
                        p.dma("pool", lambda e, gb=gb, pr=pr: e.dma_start(
                            out=u_sb[gb][:, :], in_=wu_in[pr]),
                            writes=[("usb", gb)])
                    i0 = pr * 512
                    if not (half == 1 and pr == 15):
                        p.dma("pool", lambda e, pr=pr, i0=i0: e.dma_start(
                            out=d_sb[:, :].rearrange("p (s n) -> p s n", s=4),
                            in_=wd_in[i0:i0 + 512, :].rearrange(
                                "(s p) n -> p s n", p=128)),
                            writes=["dsb"])
                    for q in range(8):
                        for c in range(C16):
                            p.op("pe", lambda e, c=c, q=q, gb=gb: e.matmul(
                                ps_g[:, :],
                                xnh[:, c * 1024 + q * 128:
                                    c * 1024 + (q + 1) * 128],
                                g_sb[gb][:, c * 512:(c + 1) * 512],
                                start=(c == 0), stop=(c == C16 - 1)),
                                reads=[("gsb", gb)] + xnh_keys, writes=["psg"])
                            p.op("pe", lambda e, c=c, q=q, gb=gb: e.matmul(
                                ps_u[:, :],
                                xnh[:, c * 1024 + q * 128:
                                    c * 1024 + (q + 1) * 128],
                                u_sb[gb][:, c * 512:(c + 1) * 512],
                                start=(c == 0), stop=(c == C16 - 1)),
                                reads=[("usb", gb)] + xnh_keys, writes=["psu"])
                        chain(q, pr)
                        if q >= 1:
                            transp_evac(q - 1, pr)
                        if q >= 2:
                            down_mms(q - 2, pr)
                    transp_evac(7, pr)
                    down_mms(6, pr)
                    down_mms(7, pr)
            for q in range(8):
                p.dma("sp", lambda e, q=q, q0=q0: e.dma_start(
                    out=out[(q0 + q) * 128:(q0 + q + 1) * 128, :],
                    in_=acc[:, q * DM:(q + 1) * DM]),
                    reads=[("acc", q)], writes=[("out", q0 + q)])
            p.fence()

    for q in range(16):
        d = p.barrier_dep(("out", q))
        p.wait("sp", d)

    with nc.Block() as block:
        p.replay(block)
    ctx.close()
    return nc


def _consts():
    j = np.arange(64, dtype=np.float64)
    freqs = (10000.0 ** (-j / 64.0))
    ang = freqs[:, None] * np.arange(S, dtype=np.float64)[None, :]
    rope = np.concatenate([np.cos(ang), np.sin(ang)],
                          axis=1).astype(ml_dtypes.bfloat16)
    ki = np.arange(128)[:, None]
    qi = np.arange(512)[None, :]
    masks = np.concatenate(
        [np.where(128 * v + ki <= qi, 1.0, 0.0) for v in range(4)],
        axis=1).astype(ml_dtypes.bfloat16)
    ident = np.eye(128).astype(ml_dtypes.bfloat16)
    return rope, masks, ident


_NC_CACHE = {}


def kernel(L0_attn_x, L0_attn_Wqkv, L0_attn_Wo, L0_ffn_w_gate, L0_ffn_w_up,
           L0_ffn_w_down):
    x = np.ascontiguousarray(np.asarray(L0_attn_x, dtype=np.float32))
    wqkv = np.ascontiguousarray(np.asarray(L0_attn_Wqkv, dtype=np.float32))
    wo = np.ascontiguousarray(np.asarray(L0_attn_Wo, dtype=np.float32))
    wg = np.ascontiguousarray(np.asarray(L0_ffn_w_gate, dtype=np.float32))
    wu = np.ascontiguousarray(np.asarray(L0_ffn_w_up, dtype=np.float32))
    wd = np.ascontiguousarray(np.asarray(L0_ffn_w_down, dtype=np.float32))

    def _re(w):
        return np.ascontiguousarray(
            w.reshape(C16, 128, 16, 512).transpose(2, 1, 0, 3)).reshape(
                16, 128, C16 * 512)
    wg_r = _re(wg)
    wu_r = _re(wu)

    rope, masks, ident = _consts()
    if "nc" not in _NC_CACHE:
        _NC_CACHE["nc"] = _build_nc()
    nc = _NC_CACHE["nc"]

    in_maps = []
    for c in range(8):
        b, h = c // 2, c % 2
        xb = np.ascontiguousarray(x[b])
        in_maps.append({
            "x_full": xb,
            "x_own": np.ascontiguousarray(xb[2048 * h: 2048 * h + 2048]),
            "wq_my": np.ascontiguousarray(wqkv[:, 1024 * h: 1024 * h + 1024]),
            "wk_my": np.ascontiguousarray(wqkv[:, 2048 + 256 * h: 2048 + 256 * h + 256]),
            "wv_my": np.ascontiguousarray(wqkv[:, 2560 + 256 * h: 2560 + 256 * h + 256]),
            "wo": np.ascontiguousarray(np.concatenate(
                [wo[1024 * h: 1024 * h + 1024],
                 wo[1024 * (1 - h): 1024 * (1 - h) + 1024]], axis=0)),
            "w_gate": wg_r,
            "w_up": wu_r,
            "w_down": wd,
            "rope": rope,
            "masks": masks,
            "ident": ident,
            "qbase": np.array([[2048 * h]], dtype=np.int32),
            "pbase": np.array([[128 * (1 - h)]], dtype=np.int32),
            "sbase": np.array([[512 * h]], dtype=np.int32),
        })
    res = run_bass_kernel_spmd(nc, in_maps, core_ids=list(range(8)))
    outp = np.empty((B, S, DM), dtype=np.float32)
    for c in range(8):
        b, h = c // 2, c % 2
        outp[b, 2048 * h: 2048 * h + 2048] = res.results[c]["out"]
    return outp


# version 14
# speedup vs baseline: 1.2169x; 1.2169x over previous
import numpy as np
import ml_dtypes
import concourse.bass as bass
import concourse.mybir as mybir
from concourse.bass_utils import run_bass_kernel_spmd

F32 = mybir.dt.float32
BF16 = mybir.dt.bfloat16
I32 = mybir.dt.int32
AF = mybir.ActivationFunctionType
ALU = mybir.AluOpType
AX = mybir.AxisListType

DM = 2048
S = 4096
B = 4
T_OWN = 2048
NH_MY = 8
D = 128
NKV_MY = 2
IC = 256
N_IC = 8192 // IC
SC = 0.08838834764831845
NEG = -2.3819763e+38
EPS = 1e-5

C16 = DM // 128


class Prog:
    ENGS = ("pe", "act", "dve", "pool", "sp")

    def __init__(self, nc, sems, slot_sems):
        self.nc = nc
        self.sem = dict(sems)
        for qn, d in slot_sems.items():
            self.sem.update(d)
        self.slot_names = {qn: list(d.keys()) for qn, d in slot_sems.items()}
        self.cnt = {e: 0 for e in self.ENGS}
        self.slot_cnt = {s: 0 for d in slot_sems.values() for s in d}
        self.slot_rr = {qn: 0 for qn in slot_sems}
        self.insts = {e: [] for e in self.ENGS}
        self.regions = {}
        self.water = {}
        self.rt = {}

    def _waits_for(self, eng, reads, writes):
        waits = []

        def need(dep):
            sname, val = dep
            if sname == eng == "pe":
                return
            w = self.water.get((eng, sname), 0)
            if val > w:
                waits.append((sname, val))
                self.water[(eng, sname)] = val
        for k in reads:
            r = self.regions.get(k)
            if r and r[0]:
                need(r[0])
        for k in writes:
            r = self.regions.get(k)
            if r:
                if r[0]:
                    need(r[0])
                for rd in r[1]:
                    need(rd)
        return waits

    def _commit(self, dep, reads, writes):
        for k in writes:
            self.regions[k] = [dep, []]
        for k in reads:
            self.regions.setdefault(k, [None, []])[1].append(dep)

    def op(self, eng, fn, reads=(), writes=()):
        waits = self._waits_for(eng, reads, writes)
        self.cnt[eng] += 1
        dep = (eng, self.cnt[eng])
        self.insts[eng].append((waits, fn, (eng, 1)))
        self._commit(dep, reads, writes)
        return dep

    def raw(self, eng, fn):
        self.insts[eng].append(([], fn, None))

    def wait(self, eng, dep):
        sname, val = dep
        w = self.water.get((eng, sname), 0)
        if val > w:
            self.insts[eng].append(([(sname, val)], None, None))
            self.water[(eng, sname)] = val

    def barrier_dep(self, key):
        r = self.regions.get(key)
        return r[0] if r else None

    def fence(self, extra=(), light=()):
        comp = [(e, self.cnt[e]) for e in self.ENGS if self.cnt[e] > 0]
        slots = [(s, c) for s, c in self.slot_cnt.items() if c > 0]
        comp += list(extra)
        for e in self.ENGS:
            deps = comp if e in light else comp + slots
            for dep in deps:
                if dep[0] == e:
                    continue
                self.wait(e, dep)

    def dma(self, queue, fn, reads=(), writes=()):
        eng = queue
        names = self.slot_names[queue]
        slot = names[self.slot_rr[queue] % len(names)]
        self.slot_rr[queue] += 1
        waits = []
        prev = self.slot_cnt[slot]
        if prev:
            w = self.water.get((eng, slot), 0)
            if prev > w:
                waits.append((slot, prev))
                self.water[(eng, slot)] = prev
        waits += self._waits_for(eng, reads, writes)
        self.slot_cnt[slot] += 16
        dep = (slot, self.slot_cnt[slot])
        self.insts[eng].append((waits, fn, (slot, 16)))
        self._commit(dep, reads, writes)
        return dep

    def replay(self, block):
        deco = {"pe": block.tensor, "act": block.scalar, "dve": block.vector,
                "pool": block.gpsimd, "sp": block.sync}
        for ename in self.ENGS:
            ilist = self.insts[ename]

            def body(eng, ilist=ilist):
                for waits, fn, inc in ilist:
                    for sname, val in waits:
                        eng.wait_ge(self.sem[sname], val)
                    if fn is None:
                        continue
                    ins = fn(eng)
                    if inc is not None and ins is not None:
                        ins.then_inc(self.sem[inc[0]], inc[1])
            deco[ename](body)


def _build_nc(sim_single=False):
    from contextlib import ExitStack
    nc = bass.Bass("TRN2", target_bir_lowering=False, num_devices=8)

    x_full = nc.dram_tensor("x_full", [S, DM], F32, kind="ExternalInput").ap()
    x_own = nc.dram_tensor("x_own", [T_OWN, DM], F32, kind="ExternalInput").ap()
    wq_in = nc.dram_tensor("wq_my", [DM, NH_MY * D], F32, kind="ExternalInput").ap()
    wk_in = nc.dram_tensor("wk_my", [DM, NKV_MY * D], F32, kind="ExternalInput").ap()
    wv_in = nc.dram_tensor("wv_my", [DM, NKV_MY * D], F32, kind="ExternalInput").ap()
    wo_in = nc.dram_tensor("wo", [DM, DM], F32, kind="ExternalInput").ap()
    wg_in = nc.dram_tensor("w_gate", [16, 128, C16 * 512], F32,
                           kind="ExternalInput").ap()
    wu_in = nc.dram_tensor("w_up", [16, 128, C16 * 512], F32,
                           kind="ExternalInput").ap()
    wd_in = nc.dram_tensor("w_down", [8192, DM], F32, kind="ExternalInput").ap()
    rope_in = nc.dram_tensor("rope", [64, 2 * S], BF16, kind="ExternalInput").ap()
    mask_in = nc.dram_tensor("masks", [128, 4 * 512], BF16, kind="ExternalInput").ap()
    id_in = nc.dram_tensor("ident", [128, 128], BF16, kind="ExternalInput").ap()
    qb_in = nc.dram_tensor("qbase", [1, 1], I32, kind="ExternalInput").ap()
    pb_in = nc.dram_tensor("pbase", [1, 1], I32, kind="ExternalInput").ap()
    sb_in = nc.dram_tensor("sbase", [1, 1], I32, kind="ExternalInput").ap()
    out = nc.dram_tensor("out", [T_OWN, DM], F32, kind="ExternalOutput").ap()

    q_dram = nc.dram_tensor("q_dram", [NH_MY * D, S], BF16, kind="Internal").ap()
    o_my = nc.dram_tensor("o_my", [NH_MY * D, S], BF16, kind="Internal").ap()
    o_allh = [nc.dram_tensor(f"o_allh{m}", [2 * D, S], BF16,
                             kind="Internal").ap() for m in range(NH_MY)]
    xn_dram = nc.dram_tensor("xn_dram", [8 * 128, C16 * 512], BF16,
                             kind="Internal").ap()
    mid = nc.dram_tensor("mid", [T_OWN, DM], F32, kind="Internal").ap()

    ctx = ExitStack()
    E = ctx.enter_context

    sems = {e: E(nc.semaphore("sem_" + e)) for e in Prog.ENGS}
    slot_sems = {
        "sp": {f"dqs{i}": E(nc.semaphore(f"dqs{i}")) for i in range(12)},
        "pool": {f"dqp{i}": E(nc.semaphore(f"dqp{i}")) for i in range(12)},
        "act": {f"dqa{i}": E(nc.semaphore(f"dqa{i}")) for i in range(8)},
    }
    cc_sem = E(nc.semaphore("cc_sem"))
    p = Prog(nc, sems, slot_sems)

    ident_sb = E(nc.sbuf_tensor("ident_sb", [128, 128], BF16))
    ones_col = E(nc.sbuf_tensor("ones_col", [128, 1], BF16))
    ones_row = E(nc.sbuf_tensor("ones_row", [1, 128], BF16))
    eps_col = E(nc.sbuf_tensor("eps_col", [128, 1], F32))
    qb_sb = E(nc.sbuf_tensor("qb_sb", [1, 1], I32))
    pb_sb = E(nc.sbuf_tensor("pb_sb", [1, 1], I32))
    sb_sb = E(nc.sbuf_tensor("sb_sb", [1, 1], I32))
    qreg = E(nc.sync.register("qreg"))
    preg = E(nc.sync.register("preg"))
    sreg = E(nc.sync.register("sreg"))

    p.dma("sp", lambda e: e.dma_start(out=ident_sb[:, :], in_=id_in[:, :]),
          writes=["ident"])
    p.dma("sp", lambda e: e.dma_start(out=qb_sb[:, :], in_=qb_in[:, :]),
          writes=["qb"])
    p.dma("sp", lambda e: e.dma_start(out=pb_sb[:, :], in_=pb_in[:, :]),
          writes=["pb"])
    p.dma("sp", lambda e: e.dma_start(out=sb_sb[:, :], in_=sb_in[:, :]),
          writes=["sb"])
    p.op("dve", lambda e: e.memset(ones_col[:, :], 1.0), writes=["ones_c"])
    p.op("dve", lambda e: e.memset(ones_row[:, :], 1.0), writes=["ones_r"])
    p.op("dve", lambda e: e.memset(eps_col[:, :], EPS), writes=["eps"])

    p.op("sp", lambda e: e.reg_load(qreg, qb_sb[:1, :1]), reads=["qb"])
    p.op("sp", lambda e: e.reg_load(preg, pb_sb[:1, :1]), reads=["pb"])
    p.op("sp", lambda e: e.reg_load(sreg, sb_sb[:1, :1]), reads=["sb"])

    def do_snap(e):
        p.rt["qoff"] = e.snap(qreg, min_val=0, max_val=T_OWN)
        p.rt["pbh"] = e.snap(preg, min_val=0, max_val=D)
        p.rt["sb"] = e.snap(sreg, min_val=0, max_val=512)
        return None
    p.raw("sp", do_snap)

    cc_groups = [[0, 1], [2, 3], [4, 5], [6, 7]]
    p.sem["cc"] = cc_sem
    cc_cnt = [0]

    def emit_head_cc(m):
        keys = [("omy", m, st) for st in range(8)]
        if sim_single:
            p.dma("pool", lambda e, m=m: e.dma_start(
                out=o_allh[m][0:D, :], in_=o_my[m * D:(m + 1) * D, :]),
                reads=keys, writes=[("oalla", m)])
            p.dma("pool", lambda e, m=m: e.dma_start(
                out=o_allh[m][D:2 * D, :], in_=o_my[m * D:(m + 1) * D, :]),
                reads=keys, writes=[("oallb", m)])
            p.regions[("oall", m)] = [p.barrier_dep(("oallb", m)), []]
        else:
            for k in keys:
                d = p.barrier_dep(k)
                if d:
                    p.wait("pool", d)
            cc_cnt[0] += 1
            p.insts["pool"].append((
                [], lambda e, m=m: e.collective_compute(
                    "AllGather", ALU.bypass, replica_groups=cc_groups,
                    ins=[o_my[m * D:(m + 1) * D, :]], outs=[o_allh[m][:, :]]),
                ("cc", 1)))
            p.regions[("oall", m)] = [("cc", cc_cnt[0]), []]

    from contextlib import ExitStack as _ES
    ctx_kv = _ES()
    kg_sb = [ctx_kv.enter_context(nc.sbuf_tensor(f"kg{g}", [128, S], BF16))
             for g in range(2)]
    vg_sb = [ctx_kv.enter_context(nc.sbuf_tensor(f"vg{g}", [128, 32 * 128],
                                                 BF16)) for g in range(2)]
    ctx_wo = _ES()
    ctx_oo = _ES()

    with ExitStack() as c1:
        E1 = c1.enter_context
        rope_sb = E1(nc.sbuf_tensor("rope_sb", [64, 2 * S], BF16))
        wq_sb = E1(nc.sbuf_tensor("wq_sb", [128, C16 * NH_MY * D], BF16))
        wk_sb = E1(nc.sbuf_tensor("wk_sb", [128, C16 * NKV_MY * D], BF16))
        wv_sb = E1(nc.sbuf_tensor("wv_sb", [128, C16 * NKV_MY * D], BF16))
        xbuf = [E1(nc.sbuf_tensor(f"xbuf{i}", [128, DM], F32)) for i in range(2)]
        xn_tok = [E1(nc.sbuf_tensor(f"xn_tok{i}", [128, DM], BF16)) for i in range(4)]
        xn_str = [E1(nc.sbuf_tensor(f"xn_str{i}", [128, C16 * 512], BF16))
                  for i in range(2)]
        s1c = [E1(nc.sbuf_tensor(f"s1c{i}", [128, 1], F32)) for i in range(2)]
        nmc = [E1(nc.sbuf_tensor(f"nmc{i}", [128, 1], F32)) for i in range(2)]
        s2c = [E1(nc.sbuf_tensor(f"s2c{i}", [128, 1], F32)) for i in range(2)]
        varc = [E1(nc.sbuf_tensor(f"varc{i}", [128, 1], F32)) for i in range(2)]
        rstdc = [E1(nc.sbuf_tensor(f"rstdc{i}", [128, 1], F32)) for i in range(2)]
        sqdump = E1(nc.sbuf_tensor("sqdump", [128, DM], BF16, side="right"))
        q_stage = [E1(nc.sbuf_tensor(f"qst{i}", [128, 512], BF16)) for i in range(2)]
        rr = [[E1(nc.sbuf_tensor(f"rr{i}_{j}", [64, 512], F32, side="right"))
               for j in range(4)] for i in range(2)]
        ps_a = [E1(nc.psum_tensor(f"ps_a{i}", [128, 512], F32)) for i in range(2)]
        ps_v = [E1(nc.psum_tensor(f"ps_v{i}", [128, NKV_MY * D], F32))
                for i in range(2)]
        ps_tr = [E1(nc.psum_tensor(f"ps_tr{i}", [128, 512], BF16)) for i in range(2)]

        p.dma("sp", lambda e: e.dma_start(out=rope_sb[:, :], in_=rope_in[:, :]),
              writes=["rope"])
        for c in range(C16):
            p.dma("pool", lambda e, c=c: e.dma_start(
                out=wq_sb[:, c * NH_MY * D:(c + 1) * NH_MY * D],
                in_=wq_in[c * 128:(c + 1) * 128, :]), writes=[("wq", c)])
            p.dma("pool", lambda e, c=c: e.dma_start(
                out=wk_sb[:, c * NKV_MY * D:(c + 1) * NKV_MY * D],
                in_=wk_in[c * 128:(c + 1) * 128, :]), writes=[("wk", c)])
            p.dma("pool", lambda e, c=c: e.dma_start(
                out=wv_sb[:, c * NKV_MY * D:(c + 1) * NKV_MY * D],
                in_=wv_in[c * 128:(c + 1) * 128, :]), writes=[("wv", c)])

        def ln_tile(t, xb, dst_key, dst_tok):
            i = xb
            p.op("dve", lambda e: e.reduce_sum(s1c[i][:, :], xbuf[i][:, :],
                                               axis=AX.X),
                 reads=[("xb", i)], writes=[("s1", i)])
            p.op("dve", lambda e: e.tensor_scalar_mul(nmc[i][:, :], s1c[i][:, :],
                                                      -1.0 / DM),
                 reads=[("s1", i)], writes=[("nm", i)])
            p.op("act", lambda e: e.activation(sqdump[:, :], xbuf[i][:, :], AF.Square,
                                               accum_out=s2c[i][:, :]),
                 reads=[("xb", i)], writes=["sqd", ("s2", i)])
            p.op("dve", lambda e: e.tensor_tensor(varc[i][:, :], nmc[i][:, :],
                                                  nmc[i][:, :], ALU.mult),
                 reads=[("nm", i)], writes=[("var", i)])
            p.op("dve", lambda e: e.tensor_scalar(varc[i][:, :], s2c[i][:, :],
                                                  1.0 / DM, varc[i][:, :],
                                                  ALU.mult, ALU.subtract),
                 reads=[("s2", i), ("var", i)], writes=[("var", i)])
            p.op("act", lambda e: e.activation(rstdc[i][:, :], varc[i][:, :], AF.Ln,
                                               bias=eps_col[:, :]),
                 reads=[("var", i), "eps"], writes=[("rstd", i)])
            p.op("act", lambda e: e.activation(rstdc[i][:, :], rstdc[i][:, :], AF.Exp,
                                               scale=-0.5),
                 reads=[("rstd", i)], writes=[("rstd", i)])
            p.op("pool", lambda e, dt=dst_tok: e.tensor_scalar(
                xn_tok[dt][:, :], xbuf[i][:, :], nmc[i][:, :], rstdc[i][:, :],
                ALU.add, ALU.mult),
                 reads=[("xb", i), ("nm", i), ("rstd", i)], writes=[dst_key])

        def transpose_tile(i, tt, str_buf):
            for g in range(4):
                pb = (tt * 4 + g) % 2
                for j in range(4):
                    c = g * 4 + j
                    p.op("pe", lambda e, c=c, j=j, pb=pb, i=i: e.transpose(
                        ps_tr[pb][:, j * 128:(j + 1) * 128],
                        xn_tok[i][:, c * 128:(c + 1) * 128], ident_sb[:, :]),
                        reads=[("xntok", i), "ident"], writes=[("pstr", pb)])
                c0 = g * 4
                p.op("act", lambda e, c0=c0, tt=tt, pb=pb, sbf=str_buf:
                     e.copy(xn_str[sbf][:, :].rearrange("p (c n) -> p c n", c=C16)
                            [:, c0:c0 + 4, tt * 128:(tt + 1) * 128],
                            ps_tr[pb][:, :].rearrange("p (c n) -> p c n", c=4)),
                     reads=[("pstr", pb)], writes=[("xns", str_buf, c0 + j, tt)
                                                   for j in range(4)])

        def rope_apply(ps, stage_top, stage_bot, s0, reads, wtop, wbot, pq):
            cosv = rope_sb[0:64, s0:s0 + 512]
            sinv = rope_sb[0:64, S + s0:S + s0 + 512]
            A = ps[0:64, :]
            Bp = ps[64:128, :]
            for j, (src_, trig) in enumerate(
                    ((A, cosv), (Bp, sinv), (A, sinv), (Bp, cosv))):
                p.op("dve", lambda e, j=j, src_=src_, trig=trig: e.tensor_tensor(
                    rr[pq][j][:, :], src_, trig, ALU.mult),
                    reads=list(reads) + ["rope"], writes=[("rr", pq, j)])
            p.op("pool", lambda e, st_=stage_top: e.tensor_tensor(
                st_, rr[pq][0][:, :], rr[pq][1][:, :], ALU.subtract),
                reads=[("rr", pq, 0), ("rr", pq, 1)], writes=wtop)
            p.op("pool", lambda e, sb_=stage_bot: e.tensor_tensor(
                sb_, rr[pq][2][:, :], rr[pq][3][:, :], ALU.add),
                reads=[("rr", pq, 2), ("rr", pq, 3)], writes=wbot)

        def ln_parts(st):
            for tt in range(4):
                t = st * 4 + tt
                i = t % 2
                p.dma("sp", lambda e, t=t, i=i: e.dma_start(
                    out=xbuf[i][:, :], in_=x_full[t * 128:(t + 1) * 128, :]),
                    writes=[("xb", i)])
                ln_tile(t, i, ("xntok", tt), tt)

        def transposes(st):
            sbf = st % 2
            for tt in range(4):
                transpose_tile(tt, tt, sbf)
            p.dma("act", lambda e, st=st, sbf=sbf: e.dma_start(
                out=xn_dram[st * 128:(st + 1) * 128, :], in_=xn_str[sbf][:, :]),
                reads=[("xns", sbf, c, tt) for c in range(C16)
                       for tt in range(4)],
                writes=[("xnd", st)])

        def projections(st):
            sbf = st % 2
            s0 = st * 512
            xs_keys = [("xns", sbf, c, tt) for c in range(C16) for tt in range(4)]
            for m in range(NH_MY):
                pq = m % 2
                for c in range(C16):
                    p.op("pe", lambda e, m=m, c=c, pq=pq, sbf=sbf: e.matmul(
                        ps_a[pq][:, :],
                        wq_sb[:, c * NH_MY * D + m * D: c * NH_MY * D + (m + 1) * D],
                        xn_str[sbf][:, c * 512:(c + 1) * 512],
                        start=(c == 0), stop=(c == C16 - 1)),
                        reads=[("wq", c)] + xs_keys, writes=[("psa", pq)])
                rope_apply(ps_a[pq], q_stage[pq][0:64, :], q_stage[pq][64:128, :],
                           s0, reads=[("psa", pq)], wtop=[("qsT", pq)],
                           wbot=[("qsB", pq)], pq=pq)
                p.dma("sp", lambda e, m=m, pq=pq, s0=s0: e.dma_start(
                    out=q_dram[m * 128:(m + 1) * 128, s0:s0 + 512],
                    in_=q_stage[pq][:, :]),
                    reads=[("qsT", pq), ("qsB", pq)], writes=[("qd", m, st)])
            for g in range(NKV_MY):
                pq = g % 2
                for c in range(C16):
                    p.op("pe", lambda e, g=g, c=c, pq=pq, sbf=sbf: e.matmul(
                        ps_a[pq][:, :],
                        wk_sb[:, c * NKV_MY * D + g * D: c * NKV_MY * D + (g + 1) * D],
                        xn_str[sbf][:, c * 512:(c + 1) * 512],
                        start=(c == 0), stop=(c == C16 - 1)),
                        reads=[("wk", c)] + xs_keys, writes=[("psa", pq)])
                rope_apply(ps_a[pq], kg_sb[g][0:64, s0:s0 + 512],
                           kg_sb[g][64:128, s0:s0 + 512], s0,
                           reads=[("psa", pq)], wtop=[("kgT", g, st)],
                           wbot=[("kgB", g, st)], pq=pq)
            for tt in range(4):
                pv = tt % 2
                jt = st * 4 + tt
                for c in range(C16):
                    p.op("pe", lambda e, tt=tt, c=c, pv=pv, sbf=sbf: e.matmul(
                        ps_v[pv][:, :],
                        xn_str[sbf][:, c * 512 + tt * 128: c * 512 + (tt + 1) * 128],
                        wv_sb[:, c * NKV_MY * D:(c + 1) * NKV_MY * D],
                        start=(c == 0), stop=(c == C16 - 1)),
                        reads=[("wv", c)] + xs_keys, writes=[("psv", pv)])
                for g in range(NKV_MY):
                    p.op("act", lambda e, pv=pv, g=g, jt=jt: e.copy(
                        vg_sb[g][:, jt * 128:(jt + 1) * 128],
                        ps_v[pv][:, g * 128:(g + 1) * 128]),
                        reads=[("psv", pv)], writes=[("vg", g, jt)])

        ln_parts(0)
        transposes(0)
        for st in range(8):
            if st + 1 < 8:
                ln_parts(st + 1)
            projections(st)
            if st + 1 < 8:
                transposes(st + 1)

    p.fence()
    wo_sb = ctx_wo.enter_context(
        nc.sbuf_tensor("wo_sb", [128, C16 * DM], BF16, side="right"))
    oo_sb = ctx_oo.enter_context(
        nc.sbuf_tensor("oo_sb", [128, C16 * T_OWN], BF16,
                       side="right"))
    for c in range(C16):
        p.dma("pool", lambda e, c=c: e.dma_start(
            out=wo_sb[:, c * DM:(c + 1) * DM],
            in_=wo_in[c * 128:(c + 1) * 128, :]), writes=[("wo", c)])

    with ExitStack() as c2:
        E2 = c2.enter_context
        m01_sb = E2(nc.sbuf_tensor("m01_sb", [128, 4 * 512], BF16))
        pt_sb = [E2(nc.sbuf_tensor(f"pt{i}", [128, 1024], BF16)) for i in range(3)]
        l_acc = [E2(nc.sbuf_tensor(f"lacc{i}", [128, 512], F32)) for i in range(2)]
        l_accp = [E2(nc.sbuf_tensor(f"laccp{i}", [128, 512], F32))
                  for i in range(2)]
        l_bf = [E2(nc.sbuf_tensor(f"lbf{i}", [128, 512], BF16)) for i in range(2)]
        rtmp = E2(nc.sbuf_tensor("rtmp", [1, 512], F32))
        rl_row = E2(nc.sbuf_tensor("rl_row", [1, 512], BF16))
        rl_sb = E2(nc.sbuf_tensor("rl_sb", [128, 512], F32))
        o_stage = [E2(nc.sbuf_tensor(f"ost{i}", [128, 512], BF16)) for i in range(2)]
        qbuf = [E2(nc.sbuf_tensor(f"qbuf{i}", [128, 512], BF16)) for i in range(2)]
        ps_w = [E2(nc.psum_tensor(f"ps_w{i}", [128, 1024], F32)) for i in range(3)]
        ps_att = [E2(nc.psum_tensor(f"ps_att{i}", [128, 512], F32)) for i in range(2)]

        p.dma("sp", lambda e: e.dma_start(out=m01_sb[:, :], in_=mask_in[:, :]),
              writes=["masks"])

        ST_ORD = [7, 0, 6, 1, 5, 2, 4, 3]
        iters = [(g, g * 4 + ml, st, si == 7) for g in range(NKV_MY)
                 for ml in range(4) for si, st in enumerate(ST_ORD)]

        def q_load(i):
            _, m, st, _l = iters[i]
            qb2 = i % 2
            p.dma("sp", lambda e, m=m, st=st, qb2=qb2: e.dma_start(
                out=qbuf[qb2][:, :],
                in_=q_dram[m * 128:(m + 1) * 128, st * 512:(st + 1) * 512]),
                reads=[("qd", m, st)], writes=[("qbuf", qb2)])

        def make_fsteps(m, st, ab, last):
            def sA(_):
                p.op("dve", lambda e, ab=ab: e.tensor_tensor(
                    l_bf[ab][:, :], l_acc[ab][:, :], l_accp[ab][:, :],
                    ALU.add),
                    reads=[("lacc", ab), ("laccp", ab)], writes=[("lbf", ab)])

            def sB1(slot):
                p.op("pe", lambda e, ab=ab, slot=slot: e.matmul(
                    ps_w[slot][0:1, 0:512], ones_col[:, :], l_bf[ab][:, :],
                    start=True, stop=True),
                    reads=[("lbf", ab), "ones_c"], writes=[("psw", slot)])
                p.op("act", lambda e, slot=slot: e.activation(
                    rtmp[:, :], ps_w[slot][0:1, 0:512], AF.Ln),
                    reads=[("psw", slot)], writes=["rtmp"])

                def _exp(e):
                    with nc.allow_low_precision("1/l feeds bf16 matmul"):
                        return e.activation(rl_row[:, :], rtmp[:, :], AF.Exp,
                                            scale=-1.0)
                p.op("act", _exp, reads=["rtmp"], writes=["rlr"])

            def sB2(slot):
                p.op("pe", lambda e, slot=slot: e.matmul(
                    ps_w[slot][:, 0:512], ones_row[:, :], rl_row[:, :],
                    start=True, stop=True),
                    reads=["rlr", "ones_r"], writes=[("psw", slot)])
                p.op("dve", lambda e, slot=slot: e.tensor_copy(
                    rl_sb[:, :], ps_w[slot][:, 0:512]),
                    reads=[("psw", slot)], writes=["rlsb"])

            def sC(_):
                p.op("dve", lambda e, ab=ab: e.tensor_tensor(
                    o_stage[ab][:, :], ps_att[ab][:, :], rl_sb[:, :], ALU.mult),
                    reads=[("psatt", ab), "rlsb"], writes=[("osta", ab)])
                p.dma("sp", lambda e, ab=ab, m=m, st=st: e.dma_start(
                    out=o_my[m * 128:(m + 1) * 128, st * 512:(st + 1) * 512],
                    in_=o_stage[ab][:, :]),
                    reads=[("osta", ab)], writes=[("omy", m, st)])
                if last:
                    cl = m
                    p.dma("sp", lambda e, cl=cl: e.dma_start(
                        out=oo_sb[:, cl * T_OWN:(cl + 1) * T_OWN],
                        in_=o_my[cl * 128:(cl + 1) * 128,
                                 bass.ds(p.rt["qoff"], T_OWN)]),
                        reads=[("omy", cl, s2_) for s2_ in range(8)],
                        writes=[("oo", cl)])
                    emit_head_cc(cl)
                    p.dma("sp", lambda e, cl=cl: e.dma_start(
                        out=oo_sb[:, (8 + cl) * T_OWN:(9 + cl) * T_OWN],
                        in_=o_allh[cl][bass.ds(p.rt["pbh"], 128),
                                       bass.ds(p.rt["qoff"], T_OWN)]),
                        reads=[("oall", cl)], writes=[("oo", 8 + cl)])
            sA(None)
            return [sB1, sB2, sC]

        fsteps = []
        q_load(0)
        for it, (g, m, st, last) in enumerate(iters):
            qb2 = it % 2
            ab = it % 2
            nkt = 4 * (st + 1)
            npairs = nkt // 2
            if it + 1 < len(iters):
                q_load(it + 1)

            def emit_pair(pp, g=g, qb2=qb2):
                pb = pp % 3
                for h in range(2):
                    kt = 2 * pp + h
                    p.op("pe", lambda e, kt=kt, h=h, pb=pb, g=g, qb2=qb2:
                         e.matmul(ps_w[pb][:, h * 512:(h + 1) * 512],
                                  kg_sb[g][:, kt * 128:(kt + 1) * 128],
                                  qbuf[qb2][:, :], start=True, stop=True),
                         reads=[("kgT", g, kt // 4), ("kgB", g, kt // 4),
                                ("qbuf", qb2)],
                         writes=[("psw", pb)])

            emit_pair(0)
            if npairs > 1:
                emit_pair(1)
            for pp in range(npairs):
                pb = pp % 3
                if pp + 2 < npairs:
                    emit_pair(pp + 2)
                p.op("act", lambda e, pb=pb: e.activation(
                    pt_sb[pb][:, :], ps_w[pb][:, :], AF.Exp, scale=SC),
                    reads=[("psw", pb)], writes=[("pt", pb)])
                if fsteps:
                    fsteps.pop(0)(pb)
                if pp >= npairs - 2:
                    v2 = pp - (npairs - 2)
                    p.op("dve", lambda e, pb=pb, v2=v2: e.tensor_tensor(
                        pt_sb[pb][:, :], pt_sb[pb][:, :],
                        m01_sb[:, v2 * 1024:(v2 + 1) * 1024], ALU.mult),
                        reads=[("pt", pb), "masks"], writes=[("pt", pb)])
                on_pool = (pp % 2 == 1)
                la = l_accp[ab] if on_pool else l_acc[ab]
                lk = ("laccp", ab) if on_pool else ("lacc", ab)
                leng = "pool" if on_pool else "dve"
                if pp <= 1:
                    p.op(leng, lambda e, pb=pb, la=la: e.tensor_tensor(
                        la[:, :], pt_sb[pb][:, 0:512],
                        pt_sb[pb][:, 512:1024], ALU.add),
                        reads=[("pt", pb)], writes=[lk])
                else:
                    for h in range(2):
                        p.op(leng, lambda e, pb=pb, la=la, h=h: e.tensor_tensor(
                            la[:, :], la[:, :],
                            pt_sb[pb][:, h * 512:(h + 1) * 512], ALU.add),
                            reads=[("pt", pb), lk], writes=[lk])
                for h in range(2):
                    kt = 2 * pp + h
                    p.op("pe", lambda e, kt=kt, h=h, pb=pb, ab=ab, g=g,
                         nkt=nkt: e.matmul(
                             ps_att[ab][:, :],
                             vg_sb[g][:, kt * 128:(kt + 1) * 128],
                             pt_sb[pb][:, h * 512:(h + 1) * 512],
                             start=(kt == 0), stop=(kt == nkt - 1)),
                         reads=[("pt", pb), ("vg", g, kt)],
                         writes=[("psatt", ab)])
            while fsteps:
                fsteps.pop(0)((npairs - 1) % 3)
            fsteps = make_fsteps(m, st, ab, last)
        while fsteps:
            fsteps.pop(0)(0)

    ctx_kv.close()

    p.fence(light=("pe",))
    with ExitStack() as c3:
        E3 = c3.enter_context
        xo_buf = [E3(nc.sbuf_tensor(f"xo{i}", [128, DM], F32)) for i in range(2)]
        out_sb = [E3(nc.sbuf_tensor(f"outsb{i}", [128, 512], F32)) for i in range(4)]
        ps_o = [E3(nc.psum_tensor(f"ps_o{i}", [128, 512], F32)) for i in range(2)]

        for qt in range(16):
            xb = qt % 2
            p.dma("sp", lambda e, qt=qt, xb=xb: e.dma_start(
                out=xo_buf[xb][:, :], in_=x_own[qt * 128:(qt + 1) * 128, :]),
                writes=[("xo", xb)])
            for n in range(4):
                ob = n % 2
                for c in range(C16):
                    p.op("pe", lambda e, qt=qt, n=n, c=c, ob=ob: e.matmul(
                        ps_o[ob][:, :],
                        oo_sb[:, c * T_OWN + qt * 128: c * T_OWN + (qt + 1) * 128],
                        wo_sb[:, c * DM + n * 512: c * DM + (n + 1) * 512],
                        start=(c == 0), stop=(c == C16 - 1)),
                        reads=[("oo", c), ("wo", c)], writes=[("pso", ob)])
                p.op("dve", lambda e, n=n, ob=ob, xb=xb: e.tensor_tensor(
                    out_sb[n][:, :], ps_o[ob][:, :],
                    xo_buf[xb][:, n * 512:(n + 1) * 512], ALU.add),
                    reads=[("pso", ob), ("xo", xb)], writes=[("outsb", n)])
                p.dma("act", lambda e, qt=qt, n=n: e.dma_start(
                    out=mid[qt * 128:(qt + 1) * 128, n * 512:(n + 1) * 512],
                    in_=out_sb[n][:, :]),
                    reads=[("outsb", n)], writes=[("mid", qt, n)])

    ctx_oo.close()
    ctx_wo.close()

    p.fence()
    with ExitStack() as c4:
        E4 = c4.enter_context
        acc = E4(nc.sbuf_tensor("acc", [128, 8 * DM], F32))
        xnh = E4(nc.sbuf_tensor("xnh", [128, C16 * 1024], BF16))
        hspt = [E4(nc.sbuf_tensor(f"hspt{q}", [128, 4 * 128], BF16))
                for q in range(3)]
        hg = [E4(nc.sbuf_tensor(f"hg{i}", [128, 512], F32)) for i in range(2)]
        aa = [E4(nc.sbuf_tensor("aa0", [128, 512], F32))] * 2
        mk = [E4(nc.sbuf_tensor("mk0", [128, 512], BF16))] * 2
        hsp = [E4(nc.sbuf_tensor(f"hsp{i}", [128, 512], BF16)) for i in range(2)]
        thr = [E4(nc.sbuf_tensor(f"thr{i}", [128, 128], F32)) for i in range(2)]
        t1 = [E4(nc.sbuf_tensor(f"t1{i}", [128, 128], F32)) for i in range(2)]
        t2 = [E4(nc.sbuf_tensor(f"t2{i}", [128, 128], F32)) for i in range(2)]
        ps_g = E4(nc.psum_tensor("ps_g", [128, 512], F32))
        ps_u = E4(nc.psum_tensor("ps_u", [128, 512], F32))
        ps_tr4 = [E4(nc.psum_tensor(f"ps_tr4{i}", [128, 512], BF16))
                  for i in range(2)]
        ps_d = [E4(nc.psum_tensor(f"ps_d{n}", [128, 512], F32)) for n in range(4)]

        def chain(q, pr):
            i = q % 2
            p.op("act", lambda e, i=i: e.activation(hg[i][:, :], ps_g[:, :],
                                                    AF.Silu),
                 reads=["psg"], writes=[("hg", i)])
            p.op("dve", lambda e, i=i: e.tensor_tensor(hg[i][:, :], hg[i][:, :],
                                                       ps_u[:, :], ALU.mult),
                 reads=[("hg", i), "psu"], writes=[("hg", i)])
            p.op("act", lambda e, i=i: e.activation(aa[i][:, :], hg[i][:, :],
                                                    AF.Abs),
                 reads=[("hg", i)], writes=["aaK"])
            a3 = aa[i].ap().rearrange("p (g j) -> p g j", j=4)
            p.op("dve", lambda e, i=i, a3=a3: e.tensor_tensor(
                t1[i][:, :], a3[:, :, 0], a3[:, :, 1], ALU.max),
                reads=["aaK"], writes=[("t1", i)])
            p.op("dve", lambda e, i=i, a3=a3: e.tensor_tensor(
                t2[i][:, :], a3[:, :, 0], a3[:, :, 1], ALU.min),
                reads=["aaK"], writes=[("t2", i)])
            p.op("dve", lambda e, i=i, a3=a3: e.tensor_tensor(
                thr[i][:, :], a3[:, :, 2], a3[:, :, 3], ALU.max),
                reads=["aaK"], writes=[("thr", i)])
            p.op("dve", lambda e, i=i: e.tensor_tensor(
                t1[i][:, :], t1[i][:, :], thr[i][:, :], ALU.min),
                reads=[("t1", i), ("thr", i)], writes=[("t1", i)])
            p.op("dve", lambda e, i=i, a3=a3: e.tensor_tensor(
                thr[i][:, :], a3[:, :, 2], a3[:, :, 3], ALU.min),
                reads=["aaK"], writes=[("thr", i)])
            p.op("dve", lambda e, i=i: e.tensor_tensor(
                t2[i][:, :], t2[i][:, :], thr[i][:, :], ALU.max),
                reads=[("t2", i), ("thr", i)], writes=[("t2", i)])
            p.op("dve", lambda e, i=i: e.tensor_tensor(
                thr[i][:, :], t1[i][:, :], t2[i][:, :], ALU.max),
                reads=[("t1", i), ("t2", i)], writes=[("thr", i)])
            thr_b = thr[i].ap().unsqueeze(2).to_broadcast((128, 128, 4))
            p.op("dve", lambda e, i=i, a3=a3, thr_b=thr_b: e.tensor_tensor(
                mk[i].ap().rearrange("p (g j) -> p g j", j=4), a3, thr_b,
                ALU.is_ge),
                reads=["aaK", ("thr", i)], writes=["mkK"])
            p.op("dve", lambda e, i=i: e.tensor_tensor(
                hsp[i][:, :], hg[i][:, :], mk[i][:, :], ALU.mult),
                reads=[("hg", i), "mkK"], writes=[("hsp", i)])

        def transp_evac(q, pr):
            i = q % 2
            tb = q % 2
            for j in range(4):
                p.op("pe", lambda e, i=i, j=j, tb=tb: e.transpose(
                    ps_tr4[tb][:, j * 128:(j + 1) * 128],
                    hsp[i][:, j * 128:(j + 1) * 128], ident_sb[:, :]),
                    reads=[("hsp", i), "ident"], writes=[("pst4", tb)])
            p.op("act", lambda e, q=q, tb=tb: e.copy(hspt[q % 3][:, :],
                                                     ps_tr4[tb][:, :]),
                 reads=[("pst4", tb)], writes=[("hspt", q)])

        def down_mms(q, pr, gb):
            for n in range(4):
                for ii in range(4):
                    p.op("pe", lambda e, q=q, n=n, ii=ii, gb=gb: e.matmul(
                        ps_d[n][:, :],
                        hspt[q % 3][:, ii * 128:(ii + 1) * 128],
                        d_sb[gb][:, ii * DM + n * 512: ii * DM + (n + 1) * 512],
                        start=(ii == 0), stop=(ii == 3)),
                        reads=[("hspt", q), ("dsb", gb)],
                        writes=[("psd", n)])
                p.op("dve", lambda e, q=q, n=n: e.tensor_tensor(
                    acc[:, q * DM + n * 512: q * DM + (n + 1) * 512],
                    ps_d[n][:, :],
                    acc[:, q * DM + n * 512: q * DM + (n + 1) * 512],
                    ALU.add),
                    reads=[("psd", n), ("acc", q)], writes=[("acc", q)])

        for half in range(2):
            q0 = half * 8
            for q in range(8):
                p.dma("sp", lambda e, q=q, q0=q0: e.dma_start(
                    out=acc[:, q * DM:(q + 1) * DM],
                    in_=mid[(q0 + q) * 128:(q0 + q + 1) * 128, :]),
                    reads=[("mid", q0 + q, n) for n in range(4)],
                    writes=[("acc", q)])
            for tt in range(2):
                p.dma("sp", lambda e, tt=tt, half=half: e.dma_start(
                    out=xnh[:, :].rearrange("p (c n) -> p c n", c=C16)
                    [:, :, tt * 512:(tt + 1) * 512],
                    in_=xn_dram[bass.ds(p.rt["sb"] + (half * 2 + tt) * 128,
                                        128), :]
                    .rearrange("p (c n) -> p c n", c=C16)),
                    reads=[("xnd", st) for st in range(8)],
                    writes=[("xnh2", half, tt)])
            xnh_keys = [("xnh2", half, tt) for tt in range(2)]
            with ExitStack() as cs:
                ES = cs.enter_context
                g_sb = [ES(nc.sbuf_tensor(f"g2_{half}_{i}", [128, C16 * 512],
                                          BF16)) for i in range(2)]
                u_sb = [ES(nc.sbuf_tensor(f"u2_{half}_{i}", [128, C16 * 512],
                                          BF16)) for i in range(2)]
                d_sb = [ES(nc.sbuf_tensor(f"d2_{half}_{i}", [128, 4 * DM],
                                          BF16)) for i in range(2)]

                pr_seq = range(16) if half == 0 else range(15, -1, -1)
                for pr in pr_seq:
                    gb = pr % 2
                    if not (half == 1 and pr >= 14):
                        p.dma("pool", lambda e, gb=gb, pr=pr: e.dma_start(
                            out=g_sb[gb][:, :], in_=wg_in[pr]),
                            writes=[("gsb", gb)])
                        p.dma("pool", lambda e, gb=gb, pr=pr: e.dma_start(
                            out=u_sb[gb][:, :], in_=wu_in[pr]),
                            writes=[("usb", gb)])
                    i0 = pr * 512
                    if not (half == 1 and pr >= 14):
                        p.dma("pool", lambda e, gb=gb, pr=pr, i0=i0:
                              e.dma_start(
                                  out=d_sb[gb][:, :].rearrange(
                                      "p (s n) -> p s n", s=4),
                                  in_=wd_in[i0:i0 + 512, :].rearrange(
                                      "(s p) n -> p s n", p=128)),
                              writes=[("dsb", gb)])
                    for q in range(8):
                        for c in range(C16):
                            p.op("pe", lambda e, c=c, q=q, gb=gb: e.matmul(
                                ps_g[:, :],
                                xnh[:, c * 1024 + q * 128:
                                    c * 1024 + (q + 1) * 128],
                                g_sb[gb][:, c * 512:(c + 1) * 512],
                                start=(c == 0), stop=(c == C16 - 1)),
                                reads=[("gsb", gb)] + xnh_keys, writes=["psg"])
                            p.op("pe", lambda e, c=c, q=q, gb=gb: e.matmul(
                                ps_u[:, :],
                                xnh[:, c * 1024 + q * 128:
                                    c * 1024 + (q + 1) * 128],
                                u_sb[gb][:, c * 512:(c + 1) * 512],
                                start=(c == 0), stop=(c == C16 - 1)),
                                reads=[("usb", gb)] + xnh_keys, writes=["psu"])
                        chain(q, pr)
                        if q >= 1:
                            transp_evac(q - 1, pr)
                        if q >= 2:
                            down_mms(q - 2, pr, gb)
                    transp_evac(7, pr)
                    down_mms(6, pr, gb)
                    down_mms(7, pr, gb)
            for q in range(8):
                p.dma("sp", lambda e, q=q, q0=q0: e.dma_start(
                    out=out[(q0 + q) * 128:(q0 + q + 1) * 128, :],
                    in_=acc[:, q * DM:(q + 1) * DM]),
                    reads=[("acc", q)], writes=[("out", q0 + q)])
            p.fence()

    for q in range(16):
        d = p.barrier_dep(("out", q))
        p.wait("sp", d)

    with nc.Block() as block:
        p.replay(block)
    ctx.close()
    return nc


def _consts():
    j = np.arange(64, dtype=np.float64)
    freqs = (10000.0 ** (-j / 64.0))
    ang = freqs[:, None] * np.arange(S, dtype=np.float64)[None, :]
    rope = np.concatenate([np.cos(ang), np.sin(ang)],
                          axis=1).astype(ml_dtypes.bfloat16)
    ki = np.arange(128)[:, None]
    qi = np.arange(512)[None, :]
    masks = np.concatenate(
        [np.where(128 * v + ki <= qi, 1.0, 0.0) for v in range(4)],
        axis=1).astype(ml_dtypes.bfloat16)
    ident = np.eye(128).astype(ml_dtypes.bfloat16)
    return rope, masks, ident


_NC_CACHE = {}


def kernel(L0_attn_x, L0_attn_Wqkv, L0_attn_Wo, L0_ffn_w_gate, L0_ffn_w_up,
           L0_ffn_w_down):
    x = np.ascontiguousarray(np.asarray(L0_attn_x, dtype=np.float32))
    wqkv = np.ascontiguousarray(np.asarray(L0_attn_Wqkv, dtype=np.float32))
    wo = np.ascontiguousarray(np.asarray(L0_attn_Wo, dtype=np.float32))
    wg = np.ascontiguousarray(np.asarray(L0_ffn_w_gate, dtype=np.float32))
    wu = np.ascontiguousarray(np.asarray(L0_ffn_w_up, dtype=np.float32))
    wd = np.ascontiguousarray(np.asarray(L0_ffn_w_down, dtype=np.float32))

    def _re(w):
        return np.ascontiguousarray(
            w.reshape(C16, 128, 16, 512).transpose(2, 1, 0, 3)).reshape(
                16, 128, C16 * 512)
    wg_r = _re(wg)
    wu_r = _re(wu)

    rope, masks, ident = _consts()
    if "nc" not in _NC_CACHE:
        _NC_CACHE["nc"] = _build_nc()
    nc = _NC_CACHE["nc"]

    in_maps = []
    for c in range(8):
        b, h = c // 2, c % 2
        xb = np.ascontiguousarray(x[b])
        in_maps.append({
            "x_full": xb,
            "x_own": np.ascontiguousarray(xb[2048 * h: 2048 * h + 2048]),
            "wq_my": np.ascontiguousarray(wqkv[:, 1024 * h: 1024 * h + 1024]),
            "wk_my": np.ascontiguousarray(wqkv[:, 2048 + 256 * h: 2048 + 256 * h + 256]),
            "wv_my": np.ascontiguousarray(wqkv[:, 2560 + 256 * h: 2560 + 256 * h + 256]),
            "wo": np.ascontiguousarray(np.concatenate(
                [wo[1024 * h: 1024 * h + 1024],
                 wo[1024 * (1 - h): 1024 * (1 - h) + 1024]], axis=0)),
            "w_gate": wg_r,
            "w_up": wu_r,
            "w_down": wd,
            "rope": rope,
            "masks": masks,
            "ident": ident,
            "qbase": np.array([[2048 * h]], dtype=np.int32),
            "pbase": np.array([[128 * (1 - h)]], dtype=np.int32),
            "sbase": np.array([[512 * h]], dtype=np.int32),
        })
    res = run_bass_kernel_spmd(nc, in_maps, core_ids=list(range(8)))
    outp = np.empty((B, S, DM), dtype=np.float32)
    for c in range(8):
        b, h = c // 2, c % 2
        outp[b, 2048 * h: 2048 * h + 2048] = res.results[c]["out"]
    return outp
